# version 37
# baseline (speedup 1.0000x reference)
import numpy as np
import ml_dtypes
import concourse.bass as bass
import concourse.mybir as mybir
from concourse.bass_utils import run_bass_kernel_spmd

F32 = mybir.dt.float32
BF16 = mybir.dt.bfloat16
FP8 = mybir.dt.float8e4
AF = mybir.ActivationFunctionType
ALU = mybir.AluOpType
ET = mybir.EngineType

D = 1024
T = 2048
S = 4096
H = 8
NKC = 32
NCH = 8
C_OFF = 12.0
EPS = 1e-5
MW = 275
f32 = np.float32

HI20, LO0, BANDC, FLAGFAR, BAND16, CORNERB = range(6)
BAND_CLASSES = (BANDC, BAND16, CORNERB)


def _attn_exp_regions(kc):
    if kc == 0:
        return [(0, T, LO0, None)]
    if 1 <= kc <= 15:
        qs = kc * 128
        return [(0, qs - 9, HI20, None), (qs - 9, T, LO0, None)]
    if kc == 16:
        return [(0, T, HI20, None)]
    return [(0, T, FLAGFAR, None)]


def _attn_band_regions(kc):
    if kc == 0:
        return [(0, 137, BANDC, 9)]
    if 1 <= kc <= 15:
        qs = kc * 128
        return [(qs - 9, qs + 137, BANDC, 0)]
    if kc == 16:
        return [(2039, T, BAND16, 0)]
    if kc <= 30:
        return []
    return [(0, 138, CORNERB, 0)]


REGS = [_attn_exp_regions(kc) for kc in range(NKC)]
BREGS = [_attn_band_regions(kc) for kc in range(NKC)]


def build_nc():
    nc = bass.Bass("TRN2", target_bir_lowering=False, detect_race_conditions=False)

    x_h = nc.dram_tensor("x", [D, T], F32, kind="ExternalInput")
    wqk_h = nc.dram_tensor("wqk", [D, D], F32, kind="ExternalInput")
    wv_h = nc.dram_tensor("wv", [D, D], F32, kind="ExternalInput")
    wout_h = nc.dram_tensor("wout", [D, D], F32, kind="ExternalInput")
    bout_h = nc.dram_tensor("bout2d", [128, 8], F32, kind="ExternalInput")
    wffn_h = nc.dram_tensor("wffn", [D, D], BF16, kind="ExternalInput")
    fbias_h = nc.dram_tensor("fbias2d", [128, 8], BF16, kind="ExternalInput")
    scal_h = nc.dram_tensor("scal", [1, 3], F32, kind="ExternalInput")
    conv_h = nc.dram_tensor("conv", [8, 21], F32, kind="ExternalInput")
    y_h = nc.dram_tensor("y", [D, T], F32, kind="ExternalOutput")

    kmine = nc.dram_tensor("kmine", [512, T], BF16)
    vmineH = [nc.dram_tensor(f"vmine{j}", [T, 512], BF16) for j in range(2)]
    kout = nc.dram_tensor("kout", [1024, T], BF16)
    voutH = [nc.dram_tensor(f"vout{j}", [2 * T, 512], BF16) for j in range(2)]
    kpart = nc.dram_tensor("kpart", [512, T], BF16)
    vpartH = [nc.dram_tensor(f"vpart{j}", [T, 512], BF16) for j in range(2)]

    A = nc.alloc_sbuf_tensor
    X = [A(f"X{c}", [128, T], F32) for c in range(NCH)]
    q_fm = [A(f"qfm{i}", [128, T], BF16) for i in range(4)]
    wffn8 = A("wffn8", [128, NCH * D], FP8)
    warena = A("warena", [128, NCH * D], BF16)
    rowar_off, _ = nc.bump_sbuf(8192)
    rowar = nc.alloc_sbuf_tensor_at("rowar", [1, T], F32, offset=rowar_off)
    onesr_f = A("onesr_f", [1, 128], F32)
    ones_b = A("ones_b", [128, 1], BF16)
    ones_f = A("ones_f", [128, 1], F32)
    conv8 = A("conv8", [128, 21 * H], F32)
    bvec = A("bvec", [128, 6 * H], F32)
    master = A("master", [128, MW * H], F32)
    m16 = A("m16", [128, 9 * H], F32)
    mcor = A("mcor", [128, 138 * H], F32)
    fbias_sb = A("fbias_sb", [128, 8], BF16)
    fbias_f32 = A("fbias_f32", [128, 8], F32)
    bout_sb = A("bout_sb", [128, 8], F32)
    srep = A("srep", [128, 8], F32)
    ptile = A("ptile", [128, 1], F32)
    vsp_t = A("vsp_t", [128, 1], F32)

    a1_off, _ = nc.bump_sbuf(32768)
    AT = nc.alloc_sbuf_tensor_at
    xq = [AT(f"xq{c}", [128, T], FP8, offset=a1_off + c * T) for c in range(NCH)]
    xln2 = [AT(f"xln2{c}", [128, T], BF16, offset=a1_off + c * 2 * T) for c in range(NCH)]
    O_fm = [AT(f"Ofm{h}", [128, T], BF16, offset=a1_off + h * 2 * T) for h in range(H)]

    a2_off, _ = nc.bump_sbuf(49152)
    xbf = [AT(f"xbf{i}", [128, 1024], BF16, offset=a2_off + i * 2048) for i in range(2)]
    sqb = [AT(f"sqb{i}", [128, 1024], BF16, offset=a2_off + 4096 + i * 2048) for i in range(2)]
    tmpf = [AT(f"tmpf{i}", [128, 1024], F32, offset=a2_off + 8192 + i * 4096) for i in range(2)]
    xlnb = [AT(f"xlnb{i}", [128, 1024], BF16, offset=a2_off + 16384 + i * 2048) for i in range(2)]
    wstage = [AT("wstage", [128, 1024], BF16, offset=a2_off + 20480),
              AT("wstageB", [128, 1024], BF16, offset=a2_off + 45056)]
    mrep = AT("mrep", [128, T], F32, offset=a2_off + 22528)
    rrep = AT("rrep", [128, T], F32, offset=a2_off + 30720)
    dtile = AT("dtile", [128, MW], F32, offset=a2_off + 38912)
    kvstage = [AT(f"kvst{i}", [128, 512], BF16, offset=a2_off + 40960 + i * 1024)
               for i in range(4)]
    yb = [AT(f"yb{i}", [128, 512], BF16, offset=a2_off + 45056 + i * 1024) for i in range(2)]
    kbufs = [AT("kbufA", [128, S], BF16, offset=a2_off + 0),
             AT("kbufB", [128, S], BF16, offset=a2_off + 24576)]
    vbuf = [AT(f"vbuf{i}", [128, S], BF16, offset=a2_off + 8192 + i * 8192) for i in range(2)]
    pt = [AT(f"pt{i}", [128, 1024], BF16, offset=a2_off + 32768 + i * 2048)
          for i in range(4)]
    acc = AT("acc", [128, 1024], F32, offset=a2_off + 40960)
    rcprep = AT("rcprep", [128, 1024], F32, offset=a2_off + 45056)
    ttmp = [AT("ttmp", [128, 1024], BF16, offset=rowar_off + 4096),
            AT("ttmpB", [128, 1024], BF16, offset=rowar_off + 6144)]

    ps = [nc.alloc_psum_tensor(f"ps{i}", [128, 1024], F32) for i in range(4)]

    class Sem:
        def __init__(self, name):
            self.h = nc.alloc_semaphore(name)
            self.n = 0
        def inc(self, ins, k=1):
            ins.then_inc(self.h, k)
            self.n += k
            return ins

    dX = Sem("dX"); dX2 = Sem("dX2"); dW = Sem("dW"); dWF = Sem("dWF"); dC = Sem("dC")
    dKVs = [Sem(f"dKV{i}") for i in range(4)]
    dKB = [Sem(f"dKB{i}") for i in range(4)]
    dVB = [Sem(f"dVB{i}") for i in range(4)]
    dStrP = [Sem("dStrE"), Sem("dStrO")]
    dSpA = [Sem("dSpAE"), Sem("dSpAO")]
    dSpK = [Sem("dSpKE"), Sem("dSpKO")]
    dPart = Sem("dPart"); dSt = Sem("dSt")
    cc = Sem("cc")
    vP = Sem("vP"); sP = Sem("sP"); tP = Sem("tP"); gP = Sem("gP")
    tSC = Sem("tSC"); sEX = Sem("sEX"); vBD = Sem("vBD"); tAV = Sem("tAV")
    vST = Sem("vST"); tSM = Sem("tSM"); vRC = Sem("vRC"); tRP = Sem("tRP")
    vRP = Sem("vRP"); vOE = Sem("vOE"); sRC = Sem("sRC"); sRP = Sem("sRP")
    vTR = Sem("vTR")

    def W(eng, s, v):
        eng.wait_ge(s.h, v)

    def VSP(vector):
        vector.tensor_scalar_mul(vsp_t[:], vsp_t[:], 1.0)

    def clip_regions(kc, s):
        w0, w1 = s * 1024, (s + 1) * 1024
        out = []
        for (qlo, qhi, cls, mc) in REGS[kc]:
            lo, hi = max(qlo, w0), min(qhi, w1)
            if lo < hi:
                out.append((lo - w0, hi - w0,
                            cls, None if mc is None else mc + (lo - qlo)))
        return out

    def clip_bands(kc, s):
        w0, w1 = s * 1024, (s + 1) * 1024
        out = []
        for (qlo, qhi, cls, mc) in BREGS[kc]:
            lo, hi = max(qlo, w0), min(qhi, w1)
            if lo < hi:
                out.append((lo - w0, hi - w0, cls, mc + (lo - qlo)))
        return out

    span_regs = {}
    span_bands = {}
    for h in range(H):
        for s in range(2):
            for kc in range(NKC):
                span_regs[(h, s, kc)] = clip_regions(kc, s)
                span_bands[(h, s, kc)] = clip_bands(kc, s)
    nband_it = [len(span_bands[(it // 64, (it // 32) % 2, it % 32)])
                for it in range(NKC * 2 * H)]
    band_cum = np.cumsum(nband_it).tolist()

    with nc.Block() as b:
        @b.sync
        def _(sync):
            dC.inc(sync.dma_start(out=rowar[0:1, 0:168],
                                  in_=bass.AP(conv_h, 0, [[168, 1], [1, 168]])), 16)
            dC.inc(sync.dma_start(out=rowar[0:1, 200:203], in_=scal_h.ap()), 16)
            dC.inc(sync.dma_start(out=fbias_sb[:], in_=fbias_h.ap()), 16)
            dC.inc(sync.dma_start(out=bout_sb[:], in_=bout_h.ap()), 16)
            for c in range(4):
                dX.inc(sync.dma_start(out=X[c][:], in_=x_h.ap()[c * 128:(c + 1) * 128, :]), 16)

        @b.gpsimd
        def _(gpsimd):
            for c in range(NCH):
                dW.inc(gpsimd.dma_start(out=warena[:, c * D:(c + 1) * D],
                                        in_=wqk_h.ap()[c * 128:(c + 1) * 128, :]), 16)
            for c in range(4, NCH):
                dX2.inc(gpsimd.dma_start(out=X[c][:], in_=x_h.ap()[c * 128:(c + 1) * 128, :]), 16)
            gpsimd.iota(dtile[:], pattern=[[1, MW]], base=0, channel_multiplier=0,
                        allow_small_or_imprecise_dtypes=True)
            gP.inc(gpsimd.iota(ptile[:], pattern=[[0, 1]], base=0, channel_multiplier=1,
                               allow_small_or_imprecise_dtypes=True), 1)
    dX_ALL = dX.n
    dX2_ALL = dX2.n
    DC_ALL = dC.n
    G_IOTA = gP.n

    with nc.Block() as b:
        @b.vector
        def _(vector):
            vector.memset(vsp_t[:], 0.0)
            vector.memset(onesr_f[:], 1.0)
            vector.memset(ones_b[:], 1.0)
            vP.inc(vector.memset(ones_f[:], 1.0), 1)
            W(vector, dC, DC_ALL)
            vector.tensor_copy(fbias_f32[:], fbias_sb[:])
            vector.memset(rowar[0:1, 207:208], 0.0)
            vector.reciprocal(rowar[0:1, 203:204], rowar[0:1, 201:202])
            vector.tensor_mul(rowar[0:1, 204:205], rowar[0:1, 201:202], rowar[0:1, 202:203])
            vector.tensor_scalar(rowar[0:1, 205:206], rowar[0:1, 200:201], -1.0, 1.0,
                                 ALU.mult, ALU.add)
            vP.inc(vector.reciprocal(rowar[0:1, 206:207], rowar[0:1, 202:203]), 1)
            W(vector, tP, 2)
            vector.tensor_copy(srep[:, 0:8], ps[2][:, 0:8])
            vector.tensor_scalar_mul(conv8[:, 0:168], ps[0][:, 0:168], 8.0)
            VSP(vector)
            VSP(vector)
            for h in range(H):
                c0 = conv8[:, h * 21:h * 21 + 1]
                c20 = conv8[:, h * 21 + 20:h * 21 + 21]
                vector.tensor_scalar(bvec[:, 6 * h + HI20:6 * h + HI20 + 1], c20,
                                     0.125, -C_OFF, ALU.mult, ALU.add)
                vector.tensor_scalar(bvec[:, 6 * h + LO0:6 * h + LO0 + 1], c0,
                                     0.125, -C_OFF, ALU.mult, ALU.add)
                vector.memset(bvec[:, 6 * h + BANDC:6 * h + BANDC + 1], -C_OFF)
            for h in range(H):
                c20 = conv8[:, h * 21 + 20:h * 21 + 21]
                vector.scalar_tensor_tensor(tmpf[0][:, 2 * h:2 * h + 1], c20, 0.125,
                                            srep[:, 5:6], ALU.mult, ALU.mult)
            for h in range(H):
                c0 = conv8[:, h * 21:h * 21 + 1]
                vector.scalar_tensor_tensor(tmpf[0][:, 2 * h + 1:2 * h + 2], c0, 0.125,
                                            srep[:, 0:1], ALU.mult, ALU.mult)
            for h in range(H):
                vector.tensor_add(tmpf[0][:, 16 + h:17 + h], tmpf[0][:, 2 * h:2 * h + 1],
                                  tmpf[0][:, 2 * h + 1:2 * h + 2])
            for h in range(H):
                vector.tensor_scalar_add(bvec[:, 6 * h + FLAGFAR:6 * h + FLAGFAR + 1],
                                         tmpf[0][:, 16 + h:17 + h], -C_OFF)
            for h in range(H):
                vector.tensor_scalar_add(bvec[:, 6 * h + BAND16:6 * h + BAND16 + 1],
                                         tmpf[0][:, 2 * h + 1:2 * h + 2], -C_OFF)
            for h in range(H):
                vector.tensor_scalar_add(bvec[:, 6 * h + CORNERB:6 * h + CORNERB + 1],
                                         tmpf[0][:, 2 * h:2 * h + 1], -C_OFF)
            W(vector, gP, G_IOTA)
            vector.tensor_scalar(dtile[:], dtile[:], -1.0, 9.0, ALU.mult, ALU.add)
            VSP(vector)
            vector.tensor_scalar(dtile[:], dtile[:], ptile[:, 0:1], None,
                                 ALU.add, ALU.bypass)
            VSP(vector)
            vector.tensor_scalar_min(dtile[:], dtile[:], 10.0)
            VSP(vector)
            vector.tensor_scalar_max(dtile[:], dtile[:], -10.0)
            W(vector, dX, dX_ALL)
            W(vector, dX2, dX2_ALL)
            vP.inc(vector.memset(ptile[:], 0.0), 1)

        @b.tensor
        def _(tensor):
            W(tensor, dC, DC_ALL)
            W(tensor, vP, 2)
            tP.inc(nc.tensor.matmul(ps[0][:, 0:168], onesr_f[:, :], rowar[0:1, 0:168],
                                    start=True, stop=True), 1)
            tP.inc(nc.tensor.matmul(ps[2][:, 0:8], onesr_f[:, :], rowar[0:1, 200:208],
                                    start=True, stop=True), 1)

        @b.sync
        def _(sync):
            for c in range(NCH):
                if c >= 2:
                    W(sync, sP, c - 1)
                dWF.inc(sync.dma_start(out=wstage[c % 2][:],
                                       in_=wffn_h.ap()[c * 128:(c + 1) * 128, :]), 16)

        @b.scalar
        def _(scalar):
            W(scalar, tP, 2)
            W(scalar, vP, 3)
            for c in range(NCH):
                W(scalar, dWF, 16 * (c + 1))
                sP.inc(scalar.activation(wffn8[:, c * D:(c + 1) * D], wstage[c % 2][:], AF.Copy,
                                         bias=0.0, scale=srep[:, 6:7]), 1)
    V_CONST = vP.n
    S_WFFN = sP.n

    def emit_ln(dst_kind):
        vbase, tbase, sbase = vP.n, tP.n, sP.n
        v_cast = lambda hc: vbase + 1 + hc
        v_vr = vbase + 17
        v_rows0 = vbase + 18
        v_rows = vbase + 19
        v_apply = lambda hc: vbase + 20 + hc
        t_mean = lambda hc: tbase + 1 + 2 * hc
        t_sq = lambda hc: tbase + 2 + 2 * hc
        t_rep = tbase + 33
        s_sq = lambda hc: sbase + 1 + hc
        s_r0 = sbase + 17
        s_quant = lambda hc: sbase + 18 + hc
        mrow = lambda seg: rowar[0:1, seg * 1024:(seg + 1) * 1024]
        vrow = lambda seg: rrep[0:1, seg * 1024:(seg + 1) * 1024]
        msq = lambda seg: mrep[0:1, seg * 1024:(seg + 1) * 1024]
        trow = lambda seg: tmpf[seg][0:1, 0:1024]

        with nc.Block() as b:
            @b.vector
            def _(vector):
                for hc in range(16):
                    c, hf = hc // 2, hc % 2
                    if hc >= 2:
                        W(vector, tP, t_mean(hc - 2))
                    vP.inc(vector.tensor_copy(xbf[hc % 2][:],
                                              X[c][:, hf * 1024:(hf + 1) * 1024]), 1)
                W(vector, tP, t_sq(15))
                for seg in range(2):
                    vector.tensor_scalar_mul(mrow(seg), ps[seg][0:1, 0:1024], 1.0 / D)
                for seg in range(2):
                    vector.tensor_scalar_mul(vrow(seg), ps[2 + seg][0:1, 0:1024], 1.0 / D)
                for seg in range(2):
                    vector.tensor_mul(msq(seg), mrow(seg), mrow(seg))
                for seg in range(2):
                    vector.tensor_sub(vrow(seg), vrow(seg), msq(seg))
                for seg in range(2):
                    ins = vector.tensor_scalar_add(vrow(seg), vrow(seg), EPS)
                vP.inc(ins, 1)
                W(vector, sP, s_r0)
                for seg in range(2):
                    vector.tensor_mul(trow(seg), msq(seg), msq(seg))
                for seg in range(2):
                    vector.tensor_mul(trow(seg), trow(seg), vrow(seg))
                for seg in range(2):
                    vector.tensor_scalar(trow(seg), trow(seg), -0.5, 1.5,
                                         ALU.mult, ALU.add)
                for seg in range(2):
                    vector.tensor_mul(vrow(seg), msq(seg), trow(seg))
                vP.inc(vector.tensor_mul(mrow(0), mrow(0), vrow(0)), 1)
                vP.inc(vector.tensor_mul(mrow(1), mrow(1), vrow(1)), 1)
                W(vector, tP, t_rep)
                for hc in range(17):
                    c, hf = hc // 2, hc % 2
                    if hc < 16:
                        if dst_kind == 'xq' and hc >= 4:
                            W(vector, sP, s_quant(hc - 4))
                        vector.tensor_mul(tmpf[hc % 2][:],
                                          X[c][:, hf * 1024:(hf + 1) * 1024],
                                          ps[hf][:])
                    if hc >= 1:
                        pc, phf = (hc - 1) // 2, (hc - 1) % 2
                        if dst_kind == 'xq':
                            dst = xlnb[(hc - 1) % 2][:]
                        else:
                            dst = xln2[pc][:, phf * 1024:(phf + 1) * 1024]
                        vP.inc(vector.tensor_sub(dst, tmpf[(hc - 1) % 2][:],
                                                 ps[2 + phf][:]), 1)
                assert vP.n == v_apply(15), (vP.n, v_apply(15))

            @b.tensor
            def _(tensor):
                for hc in range(16):
                    c, hf = hc // 2, hc % 2
                    W(tensor, vP, v_cast(hc))
                    ins = None
                    for qc in range(2):
                        ins = nc.tensor.matmul(ps[hf][0:1, qc * 512:(qc + 1) * 512],
                                               ones_b[:, :],
                                               xbf[hc % 2][:, qc * 512:(qc + 1) * 512],
                                               start=(c == 0), stop=(c == 7))
                    tP.inc(ins, 1)
                    W(tensor, sP, s_sq(hc))
                    ins = None
                    for qc in range(2):
                        ins = nc.tensor.matmul(ps[2 + hf][0:1, qc * 512:(qc + 1) * 512],
                                               ones_b[:, :],
                                               sqb[hc % 2][:, qc * 512:(qc + 1) * 512],
                                               start=(c == 0), stop=(c == 7))
                    tP.inc(ins, 1)
                ins = None
                for seg in range(2):
                    W(tensor, vP, v_rows0 if seg == 0 else v_rows)
                    for qc in range(2):
                        sl = slice(seg * 1024 + qc * 512, seg * 1024 + (qc + 1) * 512)
                        ins = nc.tensor.matmul(ps[seg][:, qc * 512:(qc + 1) * 512],
                                               onesr_f[:, :], rrep[0:1, sl],
                                               start=True, stop=True)
                        ins = nc.tensor.matmul(ps[2 + seg][:, qc * 512:(qc + 1) * 512],
                                               onesr_f[:, :], rowar[0:1, sl],
                                               start=True, stop=True)
                tP.inc(ins, 1)
                assert tP.n == t_rep

            @b.scalar
            def _(scalar):
                for hc in range(16):
                    c, hf = hc // 2, hc % 2
                    if hc >= 2:
                        W(scalar, tP, t_sq(hc - 2))
                    sP.inc(scalar.activation(sqb[hc % 2][:],
                                             X[c][:, hf * 1024:(hf + 1) * 1024],
                                             AF.Square, bias=0.0,
                                             scale=f32(1.0)), 1)
                W(scalar, vP, v_vr)
                for seg in range(2):
                    scalar.activation(msq(seg), vrow(seg), AF.Ln,
                                      bias=0.0, scale=f32(1.0))
                ins = None
                for seg in range(2):
                    ins = scalar.activation(msq(seg), msq(seg), AF.Exp,
                                            bias=0.0, scale=f32(-0.5))
                sP.inc(ins, 1)
                if dst_kind == 'xq':
                    for hc in range(16):
                        c, hf = hc // 2, hc % 2
                        W(scalar, vP, v_apply(hc))
                        sP.inc(scalar.activation(xq[c][:, hf * 1024:(hf + 1) * 1024],
                                                 xlnb[hc % 2][:], AF.Copy,
                                                 bias=0.0, scale=srep[:, 3:4]), 1)
        return {'v_apply': v_apply}

    def emit_ffn(entry_sP=None, store_y=False):
        vbase, tbase, sbase = vP.n, tP.n, sP.n
        t_ffn = lambda i: tbase + 1 + i
        s_ffn = lambda i: sbase + 1 + i
        v_ffn = lambda i: vbase + 1 + i
        slots = [(0, 0), (0, 512), (1, 0), (1, 512)]

        with nc.Block() as b:
            @b.tensor
            def _(tensor):
                if entry_sP is not None:
                    W(tensor, sP, entry_sP)
                for i in range(32):
                    ft, qc = i // 4, i % 4
                    pi, po = slots[i % 4]
                    if i >= 4:
                        W(tensor, vP, v_ffn(i - 4))
                    ins = None
                    for c in range(NCH):
                        ins = nc.tensor.matmul(
                            ps[pi][:, po:po + 512],
                            wffn8[:, c * D + ft * 128:c * D + ft * 128 + 128],
                            xq[c][:, qc * 512:(qc + 1) * 512],
                            start=(c == 0), stop=(c == 7))
                    tP.inc(ins, 1)

            @b.scalar
            def _(scalar):
                for i in range(32):
                    pi, po = slots[i % 4]
                    W(scalar, tP, t_ffn(i))
                    if i >= 2:
                        W(scalar, vP, v_ffn(i - 2))
                    sP.inc(scalar.activation(yb[i % 2][:], ps[pi][:, po:po + 512],
                                             AF.Copy, bias=0.0, scale=srep[:, 4:5]), 1)

            @b.vector
            def _(vector):
                for i in range(32):
                    ft, qc = i // 4, i % 4
                    W(vector, sP, s_ffn(i))
                    vector.tensor_scalar_add(yb[i % 2][:], yb[i % 2][:],
                                             fbias_f32[:, ft:ft + 1])
                    VSP(vector)
                    vP.inc(vector.scalar_tensor_tensor(
                        X[ft][:, qc * 512:(qc + 1) * 512], yb[i % 2][:], 0.5,
                        X[ft][:, qc * 512:(qc + 1) * 512], ALU.mult, ALU.add), 1)

            if store_y:
                @b.sync
                def _(sync):
                    for ft in range(NCH):
                        W(sync, vP, v_ffn(4 * ft + 3))
                        dSt.inc(sync.dma_start(out=y_h.ap()[ft * 128:(ft + 1) * 128, :],
                                               in_=X[ft][:]), 16)
                    W(sync, dSt, 16 * NCH)
        return {'v_ffn': v_ffn}

    ln1 = emit_ln('xq')
    ffn1 = emit_ffn()
    ln2 = emit_ln('xln2')
    V_LN2_END = vP.n

    vbase, tbase, sbase, gbase = vP.n, tP.n, sP.n, gP.n
    FORDER = [4, 5, 6, 7, 0, 1, 2, 3]
    t_qk = lambda i: tbase + 1 + i
    v_qkevac = lambda i: vbase + 1 + i
    t_v = lambda i: tbase + 32 + 1 + i
    v_vevac = lambda i: vbase + 32 + 1 + i
    vslots = [(2, 0), (2, 512), (3, 0), (3, 512)]
    qslots = [(0, 0), (0, 512), (1, 0), (1, 512)]
    with nc.Block() as b:
        @b.tensor
        def _(tensor):
            W(tensor, vP, V_LN2_END)
            W(tensor, dW, 16 * 8)
            for i in range(32):
                f, qc = FORDER[i // 4], i % 4
                pi, po = qslots[i % 4]
                if i >= 4:
                    W(tensor, vP, v_qkevac(i - 4))
                ins = None
                for c in range(NCH):
                    ins = nc.tensor.matmul(
                        ps[pi][:, po:po + 512],
                        warena[:, c * D + f * 128:c * D + f * 128 + 128],
                        xln2[c][:, qc * 512:(qc + 1) * 512],
                        start=(c == 0), stop=(c == 7))
                tP.inc(ins, 1)
            W(tensor, dW, 16 * 16)
            for i in range(32):
                fc, t_ = i // 16, i % 16
                pi, po = vslots[i % 4]
                if i >= 4:
                    W(tensor, vP, v_vevac(i - 4))
                ins = None
                for c in range(NCH):
                    ins = nc.tensor.matmul(
                        ps[pi][:, po:po + 512],
                        xln2[c][:, t_ * 128:(t_ + 1) * 128],
                        warena[:, c * D + fc * 512:c * D + (fc + 1) * 512],
                        start=(c == 0), stop=(c == 7))
                tP.inc(ins, 1)

        @b.vector
        def _(vector):
            vector.memset(kbufs[0][64:128, :], 0.0)
            vector.memset(kbufs[1][0:64, :], 0.0)
            for i in range(32):
                f, qc = FORDER[i // 4], i % 4
                pi, po = qslots[i % 4]
                W(vector, tP, t_qk(i))
                if f >= 4:
                    if i >= 4:
                        W(vector, dKB[i % 4], 16 * (i // 4))
                    vP.inc(vector.tensor_copy(kvstage[i % 4][:], ps[pi][:, po:po + 512]), 1)
                else:
                    vP.inc(vector.tensor_copy(q_fm[f][:, qc * 512:(qc + 1) * 512],
                                              ps[pi][:, po:po + 512]), 1)
            for i in range(32):
                pi, po = vslots[i % 4]
                W(vector, tP, t_v(i))
                if i < 4:
                    W(vector, dKB[i % 4], 16 * 4)
                else:
                    W(vector, dVB[i % 4], 16 * (i // 4))
                vP.inc(vector.tensor_copy(kvstage[i % 4][:], ps[pi][:, po:po + 512]), 1)

        @b.sync
        def _(sync):
            for i in range(16):
                f, qc = FORDER[i // 4], i % 4
                W(sync, vP, v_qkevac(i))
                dst = bass.AP(kmine, (f - 4) * 128 * T + qc * 512, [[T, 128], [1, 512]])
                dKB[i % 4].inc(sync.dma_start(out=dst, in_=kvstage[i % 4][:]), 16)
            for i in range(32):
                fc, t_ = i // 16, i % 16
                W(sync, vP, v_vevac(i))
                dst = bass.AP(vmineH[fc], t_ * 128 * 512, [[512, 128], [1, 512]])
                dVB[i % 4].inc(sync.dma_start(out=dst, in_=kvstage[i % 4][:]), 16)

        @b.gpsimd
        def _(gpsimd):
            for s4 in range(4):
                W(gpsimd, dKB[s4], 16 * 4)
            cc.inc(gpsimd.collective_compute(
                "AllGather", ALU.bypass,
                replica_groups=[[0, 1], [2, 3], [4, 5], [6, 7]],
                ins=[kmine.ap().opt()], outs=[kout.ap().opt()]), 1)
            W(gpsimd, tP, t_qk(31))
            for c in range(NCH):
                dW.inc(gpsimd.dma_start(out=warena[:, c * D:(c + 1) * D],
                                        in_=wv_h.ap()[c * 128:(c + 1) * 128, :]), 16)
            for s4 in range(4):
                W(gpsimd, dVB[s4], 16 * 4)
            cc.inc(gpsimd.collective_compute(
                "AllGather", ALU.bypass,
                replica_groups=[[0, 1], [2, 3], [4, 5], [6, 7]],
                ins=[vmineH[0].ap().opt()], outs=[voutH[0].ap().opt()]), 1)
            for s4 in range(4):
                W(gpsimd, dVB[s4], 16 * 8)
            cc.inc(gpsimd.collective_compute(
                "AllGather", ALU.bypass,
                replica_groups=[[0, 1], [2, 3], [4, 5], [6, 7]],
                ins=[vmineH[1].ap().opt()], outs=[voutH[1].ap().opt()]), 1)

    T_V31 = tP.n
    T_QK_END = tP.n
    V_QKV_END = vP.n
    DW_ALL = dW.n

    exp_done = lambda it: it + 1
    tsc_done = lambda it: it + 1
    with nc.Block() as b:
        @b.sync
        def _(sync):
            pid = nc.partition_id([ET.SP])
            parity = pid % 2
            W(sync, cc, 1)
            dPart.inc(sync.dma_start(out=kpart.ap(), in_=kout.ap()[512:1024, :],
                                     cond=(parity == 0)), 16)
            dPart.inc(sync.dma_start(out=kpart.ap(), in_=kout.ap()[0:512, :],
                                     cond=(parity == 1)), 16)

            def emit_partner_k(h):
                hp = (h % 2) * 64
                if h == 0:
                    W(sync, dPart, 32)
                dSpK[h % 2].inc(sync.dma_start(out=kbufs[h % 2][hp:hp + 64, T:S],
                                               in_=kpart.ap()[h * 64:(h + 1) * 64, :]), 16)

            def emit_partner_v(h):
                if h == 0:
                    W(sync, dPart, 64)
                if h == 4:
                    W(sync, dPart, 96)
                vdst2 = bass.AP(vbuf[h % 2], T, [[S, 128], [128, 16], [1, 128]])
                vsrc2 = bass.AP(vpartH[h // 4], (h % 4) * 128,
                                [[512, 128], [128 * 512, 16], [1, 128]])
                dSpA[h % 2].inc(sync.dma_start(out=vdst2, in_=vsrc2), 16)

            for h in range(H):
                hp = (h % 2) * 64
                if h >= 2:
                    W(sync, tSC, (h - 1) * 2 * NKC)
                    W(sync, tAV, (h - 1) * 2 * NKC)
                if h == 0:
                    for s4 in range(4):
                        W(sync, dKB[s4], 16 * 4)
                    for s4 in range(4):
                        W(sync, dVB[s4], 16 * 8)
                dsp = dStrP[h % 2]
                dsp.inc(sync.dma_start(out=kbufs[h % 2][hp:hp + 64, 0:T],
                                       in_=kmine.ap()[h * 64:(h + 1) * 64, :]), 16)
                vdst = bass.AP(vbuf[h % 2], 0, [[S, 128], [128, 16], [1, 128]])
                vsrc = bass.AP(vmineH[h // 4], (h % 4) * 128,
                               [[512, 128], [128 * 512, 16], [1, 128]])
                dsp.inc(sync.dma_start(out=vdst, in_=vsrc), 16)
                if h == 1:
                    emit_partner_k(0)
                    W(sync, cc, 2)
                    dPart.inc(sync.dma_start(out=vpartH[0].ap(),
                                             in_=voutH[0].ap()[T:2 * T, :],
                                             cond=(parity == 0)), 16)
                    dPart.inc(sync.dma_start(out=vpartH[0].ap(),
                                             in_=voutH[0].ap()[0:T, :],
                                             cond=(parity == 1)), 16)
                    emit_partner_v(0)
                elif h == 2:
                    emit_partner_k(1)
                    W(sync, cc, 3)
                    dPart.inc(sync.dma_start(out=vpartH[1].ap(),
                                             in_=voutH[1].ap()[T:2 * T, :],
                                             cond=(parity == 0)), 16)
                    dPart.inc(sync.dma_start(out=vpartH[1].ap(),
                                             in_=voutH[1].ap()[0:T, :],
                                             cond=(parity == 1)), 16)
                    emit_partner_v(1)
                elif h >= 3:
                    emit_partner_k(h - 1)
                    emit_partner_v(h - 1)
            emit_partner_k(7)
            emit_partner_v(7)


        @b.tensor
        def _(tensor):
            W(tensor, vP, V_QKV_END)
            for h in range(H):
                hp = (h % 2) * 64
                for s in range(2):
                    span = h * 2 + s
                    for kc in range(NKC):
                        it = span * NKC + kc
                        if it >= 2:
                            W(tensor, sEX, exp_done(it - 2))
                        if kc == 0:
                            if s == 0:
                                W(tensor, dStrP[h % 2], 32 * (h // 2 + 1))
                        if kc == 16 and s == 0:
                            W(tensor, dSpK[h % 2], 16 * (h // 2 + 1))
                            W(tensor, dSpA[h % 2], 16 * (h // 2 + 1))
                        sc = ps[kc % 2]
                        ins = None
                        for qc in range(2):
                            ins = nc.tensor.matmul(
                                sc[:, qc * 512:(qc + 1) * 512],
                                kbufs[h % 2][:, kc * 128:(kc + 1) * 128],
                                q_fm[h // 2][:,
                                             s * 1024 + qc * 512:s * 1024 + (qc + 1) * 512],
                                start=True, stop=True)
                        tSC.inc(ins, 1)
                        if kc == 1 and span >= 1:
                            pot = ps[2 + span % 2]
                            W(tensor, vTR, span)
                            ins2 = None
                            for qc in range(2):
                                ins2 = nc.tensor.matmul(
                                    pot[0:1, qc * 512:(qc + 1) * 512], ones_f[:, :],
                                    acc[:, qc * 512:(qc + 1) * 512],
                                    start=True, stop=True)
                            tSM.inc(ins2, 1)
                        if kc == 2 and span >= 1:
                            pot = ps[2 + span % 2]
                            W(tensor, sRC, span)
                            ins2 = None
                            for qc in range(2):
                                ins2 = nc.tensor.matmul(
                                    pot[:, qc * 512:(qc + 1) * 512], onesr_f[:, :],
                                    rowar[0:1, qc * 512:(qc + 1) * 512],
                                    start=True, stop=True)
                            tRP.inc(ins2, 1)
                        if kc >= 3:
                            cks = [0, 1, 2] if kc == 3 else [kc - 1]
                            if kc == 3:
                                W(tensor, sRP, span)
                                if span >= 2:
                                    W(tensor, vOE, span - 1)
                            for ck in cks:
                                W(tensor, sEX, exp_done(span * NKC + ck))
                                ov = ps[2 + span % 2]
                                ins = None
                                for qc in range(2):
                                    ins = nc.tensor.matmul(
                                        ov[:, qc * 512:(qc + 1) * 512],
                                        vbuf[h % 2][:, ck * 128:(ck + 1) * 128],
                                        pt[(span * NKC + ck) % 4][:, qc * 512:(qc + 1) * 512],
                                        start=(ck == 0), stop=False)
                                tAV.inc(ins, 1)
                    W(tensor, sEX, exp_done(span * NKC + 31))
                    ov = ps[2 + span % 2]
                    ins = None
                    for qc in range(2):
                        ins = nc.tensor.matmul(
                            ov[:, qc * 512:(qc + 1) * 512],
                            vbuf[h % 2][:, 31 * 128:32 * 128],
                            pt[(span * NKC + 31) % 4][:, qc * 512:(qc + 1) * 512],
                            start=False, stop=True)
                    tAV.inc(ins, 1)
                    if span == 15:
                        ot = ps[2 + (span + 1) % 2]
                        W(tensor, vTR, 16)
                        ins = None
                        for qc in range(2):
                            ins = nc.tensor.matmul(ot[0:1, qc * 512:(qc + 1) * 512],
                                                   ones_f[:, :],
                                                   acc[:, qc * 512:(qc + 1) * 512],
                                                   start=True, stop=True)
                        tSM.inc(ins, 1)
                        W(tensor, sRC, 16)
                        ins = None
                        for qc in range(2):
                            ins = nc.tensor.matmul(ot[:, qc * 512:(qc + 1) * 512],
                                                   onesr_f[:, :],
                                                   rowar[0:1, qc * 512:(qc + 1) * 512],
                                                   start=True, stop=True)
                        tRP.inc(ins, 1)

        @b.scalar
        def _(scalar):
            for h in range(H):
                for s in range(2):
                    span = h * 2 + s
                    for kc in range(NKC):
                        it = span * NKC + kc
                        W(scalar, tSC, tsc_done(it))
                        m = it - 4
                        if m >= 0:
                            sp_m, j = m // NKC, m % NKC
                            W(scalar, tAV, sp_m * NKC + j + 1)
                            jodd = j if j % 2 == 1 else j + 1
                            if jodd < NKC:
                                W(scalar, vST, sp_m * 16 + (jodd + 1) // 2)
                            else:
                                W(scalar, vST, (sp_m + 1) * 16)
                        if nband_it[it]:
                            W(scalar, vBD, band_cum[it])
                        regs = span_regs[(h, s, kc)]
                        ins = None
                        for (lo, hi, cls, mc) in regs:
                            ins = scalar.activation(
                                pt[it % 4][:, lo:hi], ps[kc % 2][:, lo:hi], AF.Exp,
                                bias=bvec[:, 6 * h + cls:6 * h + cls + 1],
                                scale=f32(0.125))
                        sEX.inc(ins, 1)
                        if kc == 2 and span >= 1:
                            pv = span - 1
                            pots = ps[2 + span % 2]
                            W(scalar, tSM, pv + 1)
                            sRC.inc(scalar.activation(rowar[0:1, 0:1024],
                                                      pots[0:1, 0:1024], AF.Ln,
                                                      bias=0.0, scale=f32(1.0)), 1)
                        if kc == 3 and span >= 1:
                            pv = span - 1
                            pots = ps[2 + span % 2]
                            W(scalar, tRP, pv + 1)
                            if pv >= 1:
                                W(scalar, vOE, pv)
                            sRP.inc(scalar.activation(rcprep[:], pots[:], AF.Exp,
                                                      bias=0.0, scale=f32(-1.0)), 1)
            W(scalar, tSM, 16)
            sRC.inc(scalar.activation(rowar[0:1, 0:1024],
                                      ps[2 + 16 % 2][0:1, 0:1024], AF.Ln,
                                      bias=0.0, scale=f32(1.0)), 1)
            W(scalar, tRP, 16)
            W(scalar, vOE, 15)
            sRP.inc(scalar.activation(rcprep[:], ps[2 + 16 % 2][:], AF.Exp,
                                      bias=0.0, scale=f32(-1.0)), 1)

        @b.vector
        def _(vector):
            for h in range(H):
                vector.memset(master[:, h * MW:(h + 1) * MW], 0.0)
            for t in range(21):
                vector.tensor_single_scalar(rcprep[:, 0:MW], dtile[:],
                                            float(t - 10), ALU.is_equal)
                VSP(vector)
                for h in range(H):
                    vector.scalar_tensor_tensor(
                        master[:, h * MW:(h + 1) * MW], rcprep[:, 0:MW],
                        conv8[:, h * 21 + t:h * 21 + t + 1],
                        master[:, h * MW:(h + 1) * MW], ALU.mult, ALU.add)
            for h in range(H):
                vector.tensor_scalar(m16[:, h * 9:(h + 1) * 9],
                                     master[:, h * MW:h * MW + 9],
                                     srep[:, 5:6], None, ALU.mult, ALU.bypass)
                vector.tensor_scalar(mcor[:, h * 138:(h + 1) * 138],
                                     master[:, h * MW + 137:h * MW + MW],
                                     srep[:, 0:1], None, ALU.mult, ALU.bypass)
            for h in range(H):
                c0 = conv8[:, h * 21:h * 21 + 1]
                vector.tensor_scalar_mul(dtile[:, h:h + 1], c0, -1.0)
            for h in range(H):
                c0 = conv8[:, h * 21:h * 21 + 1]
                vector.scalar_tensor_tensor(dtile[:, 8 + h:9 + h], c0, 1.0,
                                            srep[:, 0:1], ALU.mult, ALU.mult)
            for h in range(H):
                c20 = conv8[:, h * 21 + 20:h * 21 + 21]
                vector.tensor_scalar_mul(dtile[:, 16 + h:17 + h], c20, -1.0)
            for h in range(H):
                vector.tensor_scalar(master[:, h * MW:(h + 1) * MW],
                                     master[:, h * MW:(h + 1) * MW],
                                     dtile[:, h:h + 1], None, ALU.add, ALU.bypass)
            for h in range(H):
                vector.tensor_scalar(m16[:, h * 9:(h + 1) * 9],
                                     m16[:, h * 9:(h + 1) * 9],
                                     dtile[:, 8 + h:9 + h], None,
                                     ALU.add, ALU.bypass)
            for h in range(H):
                vector.tensor_scalar(m16[:, h * 9:(h + 1) * 9],
                                     m16[:, h * 9:(h + 1) * 9],
                                     dtile[:, 16 + h:17 + h], None,
                                     ALU.add, ALU.bypass)
            for h in range(H):
                vector.tensor_scalar(mcor[:, h * 138:(h + 1) * 138],
                                     mcor[:, h * 138:(h + 1) * 138],
                                     dtile[:, 8 + h:9 + h], None,
                                     ALU.subtract, ALU.bypass)
            for h in range(H):
                for s in range(2):
                    span = h * 2 + s
                    for kc in range(NKC):
                        it = span * NKC + kc
                        brs = span_bands[(h, s, kc)]
                        if brs:
                            W(vector, tSC, tsc_done(it))
                            for (lo, hi, cls, mc) in brs:
                                if cls == BANDC:
                                    msl = master[:, h * MW + mc:h * MW + mc + (hi - lo)]
                                elif cls == BAND16:
                                    msl = m16[:, h * 9 + mc:h * 9 + mc + (hi - lo)]
                                else:
                                    msl = mcor[:, h * 138 + mc:h * 138 + mc + (hi - lo)]
                                vBD.inc(vector.tensor_add(ps[kc % 2][:, lo:hi],
                                                          ps[kc % 2][:, lo:hi], msl), 1)
                        if kc == 6 and span >= 1:
                            pv = span - 1
                            W(vector, sRP, pv + 1)
                            W(vector, tAV, NKC * (pv + 1))
                            vOE.inc(vector.tensor_mul(
                                O_fm[pv // 2][:, (pv % 2) * 1024:(pv % 2 + 1) * 1024],
                                ps[2 + pv % 2][:], rcprep[:]), 1)
                        if kc % 2 == 1:
                            W(vector, sEX, exp_done(it))
                            j = kc // 2
                            if kc == 1:
                                W(vector, tSM, span)
                                vST.inc(vector.tensor_add(
                                    acc[:], pt[(it - 1) % 4][:], pt[it % 4][:]), 1)
                            else:
                                vST.inc(vector.tensor_add(ttmp[j % 2][:],
                                                          pt[(it - 1) % 4][:],
                                                          pt[it % 4][:]), 1)
                                if j >= 2:
                                    vector.scalar_tensor_tensor(
                                        acc[:], ttmp[(j - 1) % 2][:], 1.0, acc[:],
                                        ALU.mult, ALU.add)
                    vTR.inc(vector.scalar_tensor_tensor(
                        acc[:], ttmp[15 % 2][:], 1.0, acc[:], ALU.mult, ALU.add), 1)

            W(vector, sRP, 16)
            W(vector, tAV, NKC * 16)
            vOE.inc(vector.tensor_mul(O_fm[7][:, 1024:2048],
                                      ps[2 + 15 % 2][:], rcprep[:]), 1)

        @b.gpsimd
        def _(gpsimd):
            W(gpsimd, tP, T_V31)
            for c in range(NCH):
                dW.inc(gpsimd.dma_start(out=warena[:, c * D:(c + 1) * D],
                                        in_=wout_h.ap()[c * 128:(c + 1) * 128, :]), 16)

    vbase, tbase = vP.n, tP.n
    t_wo = lambda i: tbase + 1 + i
    v_wo = lambda i: vbase + 1 + i
    with nc.Block() as b:
        @b.tensor
        def _(tensor):
            W(tensor, vOE, 16)
            W(tensor, sRP, 16)
            W(tensor, dW, DW_ALL)
            for i in range(32):
                dt, qc = i // 4, i % 4
                pi, po = [(0, 0), (0, 512), (1, 0), (1, 512)][i % 4]
                if i >= 4:
                    W(tensor, vP, v_wo(i - 4))
                ins = None
                for hh in range(H):
                    ins = nc.tensor.matmul(
                        ps[pi][:, po:po + 512],
                        warena[:, hh * D + dt * 128:hh * D + dt * 128 + 128],
                        O_fm[hh][:, qc * 512:(qc + 1) * 512],
                        start=(hh == 0), stop=(hh == 7))
                tP.inc(ins, 1)

        @b.vector
        def _(vector):
            for i in range(32):
                dt, qc = i // 4, i % 4
                pi, po = [(0, 0), (0, 512), (1, 0), (1, 512)][i % 4]
                W(vector, tP, t_wo(i))
                vP.inc(vector.scalar_tensor_tensor(
                    X[dt][:, qc * 512:(qc + 1) * 512], ps[pi][:, po:po + 512],
                    bout_sb[:, dt:dt + 1], X[dt][:, qc * 512:(qc + 1) * 512],
                    ALU.add, ALU.add), 1)
    T_WOUT_END = tP.n
    V_WOUT_END = vP.n

    ln3 = emit_ln('xq')
    ffn2 = emit_ffn(store_y=True)
    V_FINAL = vP.n

    return nc


def kernel(**inputs):
    x = np.asarray(inputs["L0_attn_x"])
    wqk = np.ascontiguousarray(inputs["L0_attn_W_qk"], dtype=np.float32)
    wv = np.ascontiguousarray(inputs["L0_attn_W_v"], dtype=np.float32)
    wout = np.ascontiguousarray(inputs["L0_attn_W_out"], dtype=np.float32)
    bout = np.asarray(inputs["L0_attn_b_out"], dtype=np.float32)
    conv = np.ascontiguousarray(inputs["L0_attn_conv"], dtype=np.float32)
    wffn = np.ascontiguousarray(inputs["L0_ffn_kernel"])
    fbias = np.asarray(inputs["L0_ffn_bias"])
    sin = np.asarray(inputs["L0_ffn_input_scale"], dtype=np.float32).reshape(1)
    sk = np.asarray(inputs["L0_ffn_kernel_scale"], dtype=np.float32).reshape(1)

    bout2d = np.ascontiguousarray(bout.reshape(8, 128).T)
    fbias2d = np.ascontiguousarray(fbias.reshape(8, 128).T)

    nc = build_nc()
    in_maps = []
    for c in range(8):
        bq, hh = c // 2, c % 2
        xs = np.ascontiguousarray(x[bq, hh * T:(hh + 1) * T, :].T)
        scal = np.array([[float(c % 2), float(sin[0]), float(sk[0])]], dtype=np.float32)
        in_maps.append({
            "x": xs, "wqk": wqk, "wv": wv, "wout": wout, "bout2d": bout2d,
            "wffn": wffn, "fbias2d": fbias2d, "scal": scal, "conv": conv,
        })
    res = run_bass_kernel_spmd(nc, in_maps, core_ids=list(range(8)))
    out = np.empty((4, S, D), dtype=np.float32)
    for c in range(8):
        bq, hh = c // 2, c % 2
        out[bq, hh * T:(hh + 1) * T, :] = res.results[c]["y"].T
    return out


# version 38
# speedup vs baseline: 1.0205x; 1.0205x over previous
import numpy as np
import ml_dtypes
import concourse.bass as bass
import concourse.mybir as mybir
from concourse.bass_utils import run_bass_kernel_spmd

F32 = mybir.dt.float32
BF16 = mybir.dt.bfloat16
FP8 = mybir.dt.float8e4
AF = mybir.ActivationFunctionType
ALU = mybir.AluOpType
ET = mybir.EngineType

D = 1024
T = 2048
S = 4096
H = 8
NKC = 32
NCH = 8
C_OFF = 12.0
EPS = 1e-5
MW = 275
f32 = np.float32

HI20, LO0, BANDC, FLAGFAR, BAND16, CORNERB = range(6)
BAND_CLASSES = (BANDC, BAND16, CORNERB)


def _attn_exp_regions(kc):
    if kc == 0:
        return [(0, T, LO0, None)]
    if 1 <= kc <= 15:
        qs = kc * 128
        return [(0, qs - 9, HI20, None), (qs - 9, T, LO0, None)]
    if kc == 16:
        return [(0, T, HI20, None)]
    return [(0, T, FLAGFAR, None)]


def _attn_band_regions(kc):
    if kc == 0:
        return [(0, 137, BANDC, 9)]
    if 1 <= kc <= 15:
        qs = kc * 128
        return [(qs - 9, qs + 137, BANDC, 0)]
    if kc == 16:
        return [(2039, T, BAND16, 0)]
    if kc <= 30:
        return []
    return [(0, 138, CORNERB, 0)]


REGS = [_attn_exp_regions(kc) for kc in range(NKC)]
BREGS = [_attn_band_regions(kc) for kc in range(NKC)]


def build_nc():
    nc = bass.Bass("TRN2", target_bir_lowering=False, detect_race_conditions=False)

    x_h = nc.dram_tensor("x", [D, T], F32, kind="ExternalInput")
    wqk_h = nc.dram_tensor("wqk", [D, D], F32, kind="ExternalInput")
    wv_h = nc.dram_tensor("wv", [D, D], F32, kind="ExternalInput")
    wout_h = nc.dram_tensor("wout", [D, D], F32, kind="ExternalInput")
    bout_h = nc.dram_tensor("bout2d", [128, 8], F32, kind="ExternalInput")
    wffn_h = nc.dram_tensor("wffn", [D, D], BF16, kind="ExternalInput")
    fbias_h = nc.dram_tensor("fbias2d", [128, 8], BF16, kind="ExternalInput")
    scal_h = nc.dram_tensor("scal", [1, 3], F32, kind="ExternalInput")
    conv_h = nc.dram_tensor("conv", [8, 21], F32, kind="ExternalInput")
    y_h = nc.dram_tensor("y", [D, T], F32, kind="ExternalOutput")

    kmine = nc.dram_tensor("kmine", [512, T], BF16)
    vmineH = [nc.dram_tensor(f"vmine{j}", [T, 512], BF16) for j in range(2)]
    kout = nc.dram_tensor("kout", [1024, T], BF16)
    voutH = [nc.dram_tensor(f"vout{j}", [2 * T, 512], BF16) for j in range(2)]
    kpart = nc.dram_tensor("kpart", [512, T], BF16)
    vpartH = [nc.dram_tensor(f"vpart{j}", [T, 512], BF16) for j in range(2)]

    A = nc.alloc_sbuf_tensor
    X = [A(f"X{c}", [128, T], F32) for c in range(NCH)]
    q_fm = [A(f"qfm{i}", [128, T], BF16) for i in range(4)]
    wffn8 = A("wffn8", [128, NCH * D], FP8)
    warena = A("warena", [128, NCH * D], BF16)
    rowar_off, _ = nc.bump_sbuf(8192)
    rowar = nc.alloc_sbuf_tensor_at("rowar", [1, T], F32, offset=rowar_off)
    onesr_f = A("onesr_f", [1, 128], F32)
    ones_b = A("ones_b", [128, 1], BF16)
    ones_f = A("ones_f", [128, 1], F32)
    conv8 = A("conv8", [128, 21 * H], F32)
    bvec = A("bvec", [128, 6 * H], F32)
    master = A("master", [128, MW * H], F32)
    m16 = A("m16", [128, 9 * H], F32)
    mcor = A("mcor", [128, 138 * H], F32)
    fbias_sb = A("fbias_sb", [128, 8], BF16)
    fbias_f32 = A("fbias_f32", [128, 8], F32)
    bout_sb = A("bout_sb", [128, 8], F32)
    srep = A("srep", [128, 8], F32)
    ptile = A("ptile", [128, 1], F32)
    vsp_t = A("vsp_t", [128, 1], F32)

    a1_off, _ = nc.bump_sbuf(32768)
    AT = nc.alloc_sbuf_tensor_at
    xq = [AT(f"xq{c}", [128, T], FP8, offset=a1_off + c * T) for c in range(NCH)]
    xln2 = [AT(f"xln2{c}", [128, T], BF16, offset=a1_off + c * 2 * T) for c in range(NCH)]
    O_fm = [AT(f"Ofm{h}", [128, T], BF16, offset=a1_off + h * 2 * T) for h in range(H)]

    a2_off, _ = nc.bump_sbuf(49152)
    xbf = [AT(f"xbf{i}", [128, 1024], BF16, offset=a2_off + i * 2048) for i in range(2)]
    sqb = [AT(f"sqb{i}", [128, 1024], BF16, offset=a2_off + 4096 + i * 2048) for i in range(2)]
    tmpf = [AT(f"tmpf{i}", [128, 1024], F32, offset=a2_off + 8192 + i * 4096) for i in range(2)]
    xlnb = [AT(f"xlnb{i}", [128, 1024], BF16, offset=a2_off + 16384 + i * 2048) for i in range(2)]
    wstage = [AT("wstage", [128, 1024], BF16, offset=a2_off + 20480),
              AT("wstageB", [128, 1024], BF16, offset=a2_off + 45056)]
    mrep = AT("mrep", [128, T], F32, offset=a2_off + 22528)
    rrep = AT("rrep", [128, T], F32, offset=a2_off + 30720)
    dtile = AT("dtile", [128, MW], F32, offset=a2_off + 38912)
    kvstage = [AT(f"kvst{i}", [128, 512], BF16, offset=a2_off + 40960 + i * 1024)
               for i in range(4)]
    yb = [AT(f"yb{i}", [128, 512], BF16, offset=a2_off + 45056 + i * 1024) for i in range(2)]
    kbufs = [AT("kbufA", [128, S], BF16, offset=a2_off + 0),
             AT("kbufB", [128, S], BF16, offset=a2_off + 24576)]
    vbuf = [AT(f"vbuf{i}", [128, S], BF16, offset=a2_off + 8192 + i * 8192) for i in range(2)]
    pt = [AT(f"pt{i}", [128, 1024], BF16, offset=a2_off + 32768 + i * 2048)
          for i in range(4)]
    acc = AT("acc", [128, 1024], F32, offset=a2_off + 40960)
    rcprep = AT("rcprep", [128, 1024], F32, offset=a2_off + 45056)
    ttmp = [AT("ttmp", [128, 1024], BF16, offset=rowar_off + 4096),
            AT("ttmpB", [128, 1024], BF16, offset=rowar_off + 6144)]

    ps = [nc.alloc_psum_tensor(f"ps{i}", [128, 1024], F32) for i in range(4)]

    class Sem:
        def __init__(self, name):
            self.h = nc.alloc_semaphore(name)
            self.n = 0
        def inc(self, ins, k=1):
            ins.then_inc(self.h, k)
            self.n += k
            return ins

    dX = Sem("dX"); dX2 = Sem("dX2"); dW = Sem("dW"); dWF = Sem("dWF"); dC = Sem("dC")
    dKVs = [Sem(f"dKV{i}") for i in range(4)]
    dKB = [Sem(f"dKB{i}") for i in range(4)]
    dVB = [Sem(f"dVB{i}") for i in range(4)]
    dStrP = [Sem("dStrE"), Sem("dStrO")]
    dSpA = [Sem("dSpAE"), Sem("dSpAO")]
    dSpK = [Sem("dSpKE"), Sem("dSpKO")]
    dPart = Sem("dPart"); dSt = Sem("dSt")
    cc = Sem("cc")
    vP = Sem("vP"); sP = Sem("sP"); tP = Sem("tP"); gP = Sem("gP")
    tSC = Sem("tSC"); sEX = Sem("sEX"); vBD = Sem("vBD"); tAV = Sem("tAV")
    vST = Sem("vST"); tSM = Sem("tSM"); vRC = Sem("vRC"); tRP = Sem("tRP")
    vRP = Sem("vRP"); vOE = Sem("vOE"); sRC = Sem("sRC"); sRP = Sem("sRP")
    vTR = Sem("vTR")

    def W(eng, s, v):
        eng.wait_ge(s.h, v)

    def VSP(vector):
        vector.tensor_scalar_mul(vsp_t[:], vsp_t[:], 1.0)

    def clip_regions(kc, s):
        w0, w1 = s * 1024, (s + 1) * 1024
        out = []
        for (qlo, qhi, cls, mc) in REGS[kc]:
            lo, hi = max(qlo, w0), min(qhi, w1)
            if lo < hi:
                out.append((lo - w0, hi - w0,
                            cls, None if mc is None else mc + (lo - qlo)))
        return out

    def clip_bands(kc, s):
        w0, w1 = s * 1024, (s + 1) * 1024
        out = []
        for (qlo, qhi, cls, mc) in BREGS[kc]:
            lo, hi = max(qlo, w0), min(qhi, w1)
            if lo < hi:
                out.append((lo - w0, hi - w0, cls, mc + (lo - qlo)))
        return out

    span_regs = {}
    span_bands = {}
    for h in range(H):
        for s in range(2):
            for kc in range(NKC):
                span_regs[(h, s, kc)] = clip_regions(kc, s)
                span_bands[(h, s, kc)] = clip_bands(kc, s)
    nband_it = [len(span_bands[(it // 64, (it // 32) % 2, it % 32)])
                for it in range(NKC * 2 * H)]
    band_cum = np.cumsum(nband_it).tolist()

    with nc.Block() as b:
        @b.sync
        def _(sync):
            dC.inc(sync.dma_start(out=rowar[0:1, 0:168],
                                  in_=bass.AP(conv_h, 0, [[168, 1], [1, 168]])), 16)
            dC.inc(sync.dma_start(out=rowar[0:1, 200:203], in_=scal_h.ap()), 16)
            dC.inc(sync.dma_start(out=fbias_sb[:], in_=fbias_h.ap()), 16)
            dC.inc(sync.dma_start(out=bout_sb[:], in_=bout_h.ap()), 16)
            for c in range(4):
                dX.inc(sync.dma_start(out=X[c][:], in_=x_h.ap()[c * 128:(c + 1) * 128, :]), 16)

        @b.gpsimd
        def _(gpsimd):
            for c in range(NCH):
                dW.inc(gpsimd.dma_start(out=warena[:, c * D:(c + 1) * D],
                                        in_=wqk_h.ap()[c * 128:(c + 1) * 128, :]), 16)
            for c in range(4, NCH):
                dX2.inc(gpsimd.dma_start(out=X[c][:], in_=x_h.ap()[c * 128:(c + 1) * 128, :]), 16)
            gpsimd.iota(dtile[:], pattern=[[1, MW]], base=0, channel_multiplier=0,
                        allow_small_or_imprecise_dtypes=True)
            gP.inc(gpsimd.iota(ptile[:], pattern=[[0, 1]], base=0, channel_multiplier=1,
                               allow_small_or_imprecise_dtypes=True), 1)
    dX_ALL = dX.n
    dX2_ALL = dX2.n
    DC_ALL = dC.n
    G_IOTA = gP.n

    with nc.Block() as b:
        @b.vector
        def _(vector):
            vector.memset(vsp_t[:], 0.0)
            vector.memset(onesr_f[:], 1.0)
            vector.memset(ones_b[:], 1.0)
            vP.inc(vector.memset(ones_f[:], 1.0), 1)
            W(vector, dC, DC_ALL)
            vector.tensor_copy(fbias_f32[:], fbias_sb[:])
            vector.memset(rowar[0:1, 207:208], 0.0)
            vector.reciprocal(rowar[0:1, 203:204], rowar[0:1, 201:202])
            vector.tensor_mul(rowar[0:1, 204:205], rowar[0:1, 201:202], rowar[0:1, 202:203])
            vector.tensor_scalar(rowar[0:1, 205:206], rowar[0:1, 200:201], -1.0, 1.0,
                                 ALU.mult, ALU.add)
            vP.inc(vector.reciprocal(rowar[0:1, 206:207], rowar[0:1, 202:203]), 1)
            W(vector, tP, 2)
            vector.tensor_copy(srep[:, 0:8], ps[2][:, 0:8])
            vector.tensor_scalar_mul(conv8[:, 0:168], ps[0][:, 0:168], 8.0)
            VSP(vector)
            VSP(vector)
            for h in range(H):
                c0 = conv8[:, h * 21:h * 21 + 1]
                c20 = conv8[:, h * 21 + 20:h * 21 + 21]
                vector.tensor_scalar(bvec[:, 6 * h + HI20:6 * h + HI20 + 1], c20,
                                     0.125, -C_OFF, ALU.mult, ALU.add)
                vector.tensor_scalar(bvec[:, 6 * h + LO0:6 * h + LO0 + 1], c0,
                                     0.125, -C_OFF, ALU.mult, ALU.add)
                vector.memset(bvec[:, 6 * h + BANDC:6 * h + BANDC + 1], -C_OFF)
            for h in range(H):
                c20 = conv8[:, h * 21 + 20:h * 21 + 21]
                vector.scalar_tensor_tensor(tmpf[0][:, 2 * h:2 * h + 1], c20, 0.125,
                                            srep[:, 5:6], ALU.mult, ALU.mult)
            for h in range(H):
                c0 = conv8[:, h * 21:h * 21 + 1]
                vector.scalar_tensor_tensor(tmpf[0][:, 2 * h + 1:2 * h + 2], c0, 0.125,
                                            srep[:, 0:1], ALU.mult, ALU.mult)
            for h in range(H):
                vector.tensor_add(tmpf[0][:, 16 + h:17 + h], tmpf[0][:, 2 * h:2 * h + 1],
                                  tmpf[0][:, 2 * h + 1:2 * h + 2])
            for h in range(H):
                vector.tensor_scalar_add(bvec[:, 6 * h + FLAGFAR:6 * h + FLAGFAR + 1],
                                         tmpf[0][:, 16 + h:17 + h], -C_OFF)
            for h in range(H):
                vector.tensor_scalar_add(bvec[:, 6 * h + BAND16:6 * h + BAND16 + 1],
                                         tmpf[0][:, 2 * h + 1:2 * h + 2], -C_OFF)
            for h in range(H):
                vector.tensor_scalar_add(bvec[:, 6 * h + CORNERB:6 * h + CORNERB + 1],
                                         tmpf[0][:, 2 * h:2 * h + 1], -C_OFF)
            W(vector, gP, G_IOTA)
            vector.tensor_scalar(dtile[:], dtile[:], -1.0, 9.0, ALU.mult, ALU.add)
            VSP(vector)
            vector.tensor_scalar(dtile[:], dtile[:], ptile[:, 0:1], None,
                                 ALU.add, ALU.bypass)
            VSP(vector)
            vector.tensor_scalar_min(dtile[:], dtile[:], 10.0)
            VSP(vector)
            vector.tensor_scalar_max(dtile[:], dtile[:], -10.0)
            W(vector, dX, dX_ALL)
            W(vector, dX2, dX2_ALL)
            vP.inc(vector.memset(ptile[:], 0.0), 1)

        @b.tensor
        def _(tensor):
            W(tensor, dC, DC_ALL)
            W(tensor, vP, 2)
            tP.inc(nc.tensor.matmul(ps[0][:, 0:168], onesr_f[:, :], rowar[0:1, 0:168],
                                    start=True, stop=True), 1)
            tP.inc(nc.tensor.matmul(ps[2][:, 0:8], onesr_f[:, :], rowar[0:1, 200:208],
                                    start=True, stop=True), 1)

        @b.sync
        def _(sync):
            for c in range(NCH):
                if c >= 2:
                    W(sync, sP, c - 1)
                dWF.inc(sync.dma_start(out=wstage[c % 2][:],
                                       in_=wffn_h.ap()[c * 128:(c + 1) * 128, :]), 16)

        @b.scalar
        def _(scalar):
            W(scalar, tP, 2)
            W(scalar, vP, 3)
            for c in range(NCH):
                W(scalar, dWF, 16 * (c + 1))
                sP.inc(scalar.activation(wffn8[:, c * D:(c + 1) * D], wstage[c % 2][:], AF.Copy,
                                         bias=0.0, scale=srep[:, 6:7]), 1)
    V_CONST = vP.n
    S_WFFN = sP.n

    def emit_ln(dst_kind):
        vbase, tbase, sbase = vP.n, tP.n, sP.n
        v_cast = lambda hc: vbase + 1 + hc
        v_vr = vbase + 17
        v_rows0 = vbase + 18
        v_rows = vbase + 19
        v_apply = lambda hc: vbase + 20 + hc
        t_mean = lambda hc: tbase + 1 + 2 * hc
        t_sq = lambda hc: tbase + 2 + 2 * hc
        t_rep = tbase + 33
        s_sq = lambda hc: sbase + 1 + hc
        s_r0 = sbase + 17
        s_quant = lambda hc: sbase + 18 + hc
        mrow = lambda seg: rowar[0:1, seg * 1024:(seg + 1) * 1024]
        vrow = lambda seg: rrep[0:1, seg * 1024:(seg + 1) * 1024]
        msq = lambda seg: mrep[0:1, seg * 1024:(seg + 1) * 1024]
        trow = lambda seg: tmpf[seg][0:1, 0:1024]

        with nc.Block() as b:
            @b.vector
            def _(vector):
                for hc in range(16):
                    c, hf = hc // 2, hc % 2
                    if hc >= 2:
                        W(vector, tP, t_mean(hc - 2))
                    vP.inc(vector.tensor_copy(xbf[hc % 2][:],
                                              X[c][:, hf * 1024:(hf + 1) * 1024]), 1)
                W(vector, tP, t_sq(15))
                for seg in range(2):
                    vector.tensor_scalar_mul(mrow(seg), ps[seg][0:1, 0:1024], 1.0 / D)
                for seg in range(2):
                    vector.tensor_scalar(vrow(seg), ps[2 + seg][0:1, 0:1024],
                                         1.0 / D, EPS, ALU.mult, ALU.add)
                for seg in range(2):
                    vector.tensor_mul(msq(seg), mrow(seg), mrow(seg))
                for seg in range(2):
                    ins = vector.tensor_sub(vrow(seg), vrow(seg), msq(seg))
                vP.inc(ins, 1)
                W(vector, sP, s_r0)
                for seg in range(2):
                    vector.tensor_mul(trow(seg), msq(seg), msq(seg))
                for seg in range(2):
                    vector.tensor_mul(trow(seg), trow(seg), vrow(seg))
                for seg in range(2):
                    vector.tensor_scalar(trow(seg), trow(seg), -0.5, 1.5,
                                         ALU.mult, ALU.add)
                for seg in range(2):
                    vector.tensor_mul(vrow(seg), msq(seg), trow(seg))
                vP.inc(vector.tensor_mul(mrow(0), mrow(0), vrow(0)), 1)
                vP.inc(vector.tensor_mul(mrow(1), mrow(1), vrow(1)), 1)
                W(vector, tP, t_rep)
                for hc in range(17):
                    c, hf = hc // 2, hc % 2
                    if hc < 16:
                        if dst_kind == 'xq' and hc >= 3:
                            W(vector, sP, s_quant(hc - 3))
                        vector.tensor_mul(tmpf[hc % 2][:],
                                          X[c][:, hf * 1024:(hf + 1) * 1024],
                                          ps[hf][:])
                    if hc >= 1:
                        pc, phf = (hc - 1) // 2, (hc - 1) % 2
                        if dst_kind == 'xq':
                            dst = xlnb[(hc - 1) % 2][:]
                        else:
                            dst = xln2[pc][:, phf * 1024:(phf + 1) * 1024]
                        vP.inc(vector.tensor_sub(dst, tmpf[(hc - 1) % 2][:],
                                                 ps[2 + phf][:]), 1)
                assert vP.n == v_apply(15), (vP.n, v_apply(15))

            @b.tensor
            def _(tensor):
                for hc in range(16):
                    c, hf = hc // 2, hc % 2
                    W(tensor, vP, v_cast(hc))
                    ins = None
                    for qc in range(2):
                        ins = nc.tensor.matmul(ps[hf][0:1, qc * 512:(qc + 1) * 512],
                                               ones_b[:, :],
                                               xbf[hc % 2][:, qc * 512:(qc + 1) * 512],
                                               start=(c == 0), stop=(c == 7))
                    tP.inc(ins, 1)
                    W(tensor, sP, s_sq(hc))
                    ins = None
                    for qc in range(2):
                        ins = nc.tensor.matmul(ps[2 + hf][0:1, qc * 512:(qc + 1) * 512],
                                               ones_b[:, :],
                                               sqb[hc % 2][:, qc * 512:(qc + 1) * 512],
                                               start=(c == 0), stop=(c == 7))
                    tP.inc(ins, 1)
                ins = None
                for seg in range(2):
                    W(tensor, vP, v_rows0 if seg == 0 else v_rows)
                    for qc in range(2):
                        sl = slice(seg * 1024 + qc * 512, seg * 1024 + (qc + 1) * 512)
                        ins = nc.tensor.matmul(ps[seg][:, qc * 512:(qc + 1) * 512],
                                               onesr_f[:, :], rrep[0:1, sl],
                                               start=True, stop=True)
                        ins = nc.tensor.matmul(ps[2 + seg][:, qc * 512:(qc + 1) * 512],
                                               onesr_f[:, :], rowar[0:1, sl],
                                               start=True, stop=True)
                tP.inc(ins, 1)
                assert tP.n == t_rep

            @b.scalar
            def _(scalar):
                for hc in range(16):
                    c, hf = hc // 2, hc % 2
                    if hc >= 2:
                        W(scalar, tP, t_sq(hc - 2))
                    sP.inc(scalar.activation(sqb[hc % 2][:],
                                             X[c][:, hf * 1024:(hf + 1) * 1024],
                                             AF.Square, bias=0.0,
                                             scale=f32(1.0)), 1)
                W(scalar, vP, v_vr)
                for seg in range(2):
                    scalar.activation(msq(seg), vrow(seg), AF.Ln,
                                      bias=0.0, scale=f32(1.0))
                ins = None
                for seg in range(2):
                    ins = scalar.activation(msq(seg), msq(seg), AF.Exp,
                                            bias=0.0, scale=f32(-0.5))
                sP.inc(ins, 1)
                if dst_kind == 'xq':
                    for hc in range(16):
                        c, hf = hc // 2, hc % 2
                        W(scalar, vP, v_apply(hc))
                        sP.inc(scalar.activation(xq[c][:, hf * 1024:(hf + 1) * 1024],
                                                 xlnb[hc % 2][:], AF.Copy,
                                                 bias=0.0, scale=srep[:, 3:4]), 1)
        return {'v_apply': v_apply}

    def emit_ffn(entry_sP=None, store_y=False):
        vbase, tbase, sbase = vP.n, tP.n, sP.n
        t_ffn = lambda i: tbase + 1 + i
        s_ffn = lambda i: sbase + 1 + i
        v_ffn = lambda i: vbase + 1 + i
        slots = [(0, 0), (0, 512), (1, 0), (1, 512)]

        with nc.Block() as b:
            @b.tensor
            def _(tensor):
                if entry_sP is not None:
                    W(tensor, sP, entry_sP)
                for i in range(32):
                    ft, qc = i // 4, i % 4
                    pi, po = slots[i % 4]
                    if i >= 4:
                        W(tensor, vP, v_ffn(i - 4))
                    ins = None
                    for c in range(NCH):
                        ins = nc.tensor.matmul(
                            ps[pi][:, po:po + 512],
                            wffn8[:, c * D + ft * 128:c * D + ft * 128 + 128],
                            xq[c][:, qc * 512:(qc + 1) * 512],
                            start=(c == 0), stop=(c == 7))
                    tP.inc(ins, 1)

            @b.scalar
            def _(scalar):
                for i in range(32):
                    pi, po = slots[i % 4]
                    W(scalar, tP, t_ffn(i))
                    if i >= 2:
                        W(scalar, vP, v_ffn(i - 2))
                    sP.inc(scalar.activation(yb[i % 2][:], ps[pi][:, po:po + 512],
                                             AF.Copy, bias=0.0, scale=srep[:, 4:5]), 1)

            @b.vector
            def _(vector):
                for i in range(32):
                    ft, qc = i // 4, i % 4
                    W(vector, sP, s_ffn(i))
                    vector.tensor_scalar_add(yb[i % 2][:], yb[i % 2][:],
                                             fbias_f32[:, ft:ft + 1])
                    VSP(vector)
                    vP.inc(vector.scalar_tensor_tensor(
                        X[ft][:, qc * 512:(qc + 1) * 512], yb[i % 2][:], 0.5,
                        X[ft][:, qc * 512:(qc + 1) * 512], ALU.mult, ALU.add), 1)

            if store_y:
                @b.sync
                def _(sync):
                    for ft in range(NCH):
                        W(sync, vP, v_ffn(4 * ft + 3))
                        dSt.inc(sync.dma_start(out=y_h.ap()[ft * 128:(ft + 1) * 128, :],
                                               in_=X[ft][:]), 16)
                    W(sync, dSt, 16 * NCH)
        return {'v_ffn': v_ffn}

    ln1 = emit_ln('xq')
    ffn1 = emit_ffn()
    ln2 = emit_ln('xln2')
    V_LN2_END = vP.n

    vbase, tbase, sbase, gbase = vP.n, tP.n, sP.n, gP.n
    FORDER = [4, 5, 6, 7, 0, 1, 2, 3]
    t_qk = lambda i: tbase + 1 + i
    v_qkevac = lambda i: vbase + 1 + i
    t_v = lambda i: tbase + 32 + 1 + i
    v_vevac = lambda i: vbase + 32 + 1 + i
    vslots = [(2, 0), (2, 512), (3, 0), (3, 512)]
    qslots = [(0, 0), (0, 512), (1, 0), (1, 512)]
    with nc.Block() as b:
        @b.tensor
        def _(tensor):
            W(tensor, vP, V_LN2_END)
            W(tensor, dW, 16 * 8)
            for i in range(32):
                f, qc = FORDER[i // 4], i % 4
                pi, po = qslots[i % 4]
                if i >= 4:
                    W(tensor, vP, v_qkevac(i - 4))
                ins = None
                for c in range(NCH):
                    ins = nc.tensor.matmul(
                        ps[pi][:, po:po + 512],
                        warena[:, c * D + f * 128:c * D + f * 128 + 128],
                        xln2[c][:, qc * 512:(qc + 1) * 512],
                        start=(c == 0), stop=(c == 7))
                tP.inc(ins, 1)
            W(tensor, dW, 16 * 16)
            for i in range(32):
                fc, t_ = i // 16, i % 16
                pi, po = vslots[i % 4]
                if i >= 4:
                    W(tensor, vP, v_vevac(i - 4))
                ins = None
                for c in range(NCH):
                    ins = nc.tensor.matmul(
                        ps[pi][:, po:po + 512],
                        xln2[c][:, t_ * 128:(t_ + 1) * 128],
                        warena[:, c * D + fc * 512:c * D + (fc + 1) * 512],
                        start=(c == 0), stop=(c == 7))
                tP.inc(ins, 1)

        @b.vector
        def _(vector):
            vector.memset(kbufs[0][64:128, :], 0.0)
            vector.memset(kbufs[1][0:64, :], 0.0)
            for i in range(32):
                f, qc = FORDER[i // 4], i % 4
                pi, po = qslots[i % 4]
                W(vector, tP, t_qk(i))
                if f >= 4:
                    if i >= 4:
                        W(vector, dKB[i % 4], 16 * (i // 4))
                    vP.inc(vector.tensor_copy(kvstage[i % 4][:], ps[pi][:, po:po + 512]), 1)
                else:
                    vP.inc(vector.tensor_copy(q_fm[f][:, qc * 512:(qc + 1) * 512],
                                              ps[pi][:, po:po + 512]), 1)
            for i in range(32):
                pi, po = vslots[i % 4]
                W(vector, tP, t_v(i))
                if i < 4:
                    W(vector, dKB[i % 4], 16 * 4)
                else:
                    W(vector, dVB[i % 4], 16 * (i // 4))
                vP.inc(vector.tensor_copy(kvstage[i % 4][:], ps[pi][:, po:po + 512]), 1)

        @b.sync
        def _(sync):
            for i in range(16):
                f, qc = FORDER[i // 4], i % 4
                W(sync, vP, v_qkevac(i))
                dst = bass.AP(kmine, (f - 4) * 128 * T + qc * 512, [[T, 128], [1, 512]])
                dKB[i % 4].inc(sync.dma_start(out=dst, in_=kvstage[i % 4][:]), 16)
            for i in range(32):
                fc, t_ = i // 16, i % 16
                W(sync, vP, v_vevac(i))
                dst = bass.AP(vmineH[fc], t_ * 128 * 512, [[512, 128], [1, 512]])
                dVB[i % 4].inc(sync.dma_start(out=dst, in_=kvstage[i % 4][:]), 16)

        @b.gpsimd
        def _(gpsimd):
            for s4 in range(4):
                W(gpsimd, dKB[s4], 16 * 4)
            cc.inc(gpsimd.collective_compute(
                "AllGather", ALU.bypass,
                replica_groups=[[0, 1], [2, 3], [4, 5], [6, 7]],
                ins=[kmine.ap().opt()], outs=[kout.ap().opt()]), 1)
            W(gpsimd, tP, t_qk(31))
            for c in range(NCH):
                dW.inc(gpsimd.dma_start(out=warena[:, c * D:(c + 1) * D],
                                        in_=wv_h.ap()[c * 128:(c + 1) * 128, :]), 16)
            for s4 in range(4):
                W(gpsimd, dVB[s4], 16 * 4)
            cc.inc(gpsimd.collective_compute(
                "AllGather", ALU.bypass,
                replica_groups=[[0, 1], [2, 3], [4, 5], [6, 7]],
                ins=[vmineH[0].ap().opt()], outs=[voutH[0].ap().opt()]), 1)
            for s4 in range(4):
                W(gpsimd, dVB[s4], 16 * 8)
            cc.inc(gpsimd.collective_compute(
                "AllGather", ALU.bypass,
                replica_groups=[[0, 1], [2, 3], [4, 5], [6, 7]],
                ins=[vmineH[1].ap().opt()], outs=[voutH[1].ap().opt()]), 1)

    T_V31 = tP.n
    T_QK_END = tP.n
    V_QKV_END = vP.n
    DW_ALL = dW.n

    exp_done = lambda it: it + 1
    tsc_done = lambda it: it + 1
    with nc.Block() as b:
        @b.sync
        def _(sync):
            pid = nc.partition_id([ET.SP])
            parity = pid % 2
            W(sync, cc, 1)
            dPart.inc(sync.dma_start(out=kpart.ap(), in_=kout.ap()[512:1024, :],
                                     cond=(parity == 0)), 16)
            dPart.inc(sync.dma_start(out=kpart.ap(), in_=kout.ap()[0:512, :],
                                     cond=(parity == 1)), 16)

            def emit_partner_k(h):
                hp = (h % 2) * 64
                if h == 0:
                    W(sync, dPart, 32)
                dSpK[h % 2].inc(sync.dma_start(out=kbufs[h % 2][hp:hp + 64, T:S],
                                               in_=kpart.ap()[h * 64:(h + 1) * 64, :]), 16)

            def emit_partner_v(h):
                if h == 0:
                    W(sync, dPart, 64)
                if h == 4:
                    W(sync, dPart, 96)
                vdst2 = bass.AP(vbuf[h % 2], T, [[S, 128], [128, 16], [1, 128]])
                vsrc2 = bass.AP(vpartH[h // 4], (h % 4) * 128,
                                [[512, 128], [128 * 512, 16], [1, 128]])
                dSpA[h % 2].inc(sync.dma_start(out=vdst2, in_=vsrc2), 16)

            for h in range(H):
                hp = (h % 2) * 64
                if h >= 2:
                    W(sync, tSC, (h - 1) * 2 * NKC)
                    W(sync, tAV, (h - 1) * 2 * NKC)
                if h == 0:
                    for s4 in range(4):
                        W(sync, dKB[s4], 16 * 4)
                    for s4 in range(4):
                        W(sync, dVB[s4], 16 * 8)
                dsp = dStrP[h % 2]
                dsp.inc(sync.dma_start(out=kbufs[h % 2][hp:hp + 64, 0:T],
                                       in_=kmine.ap()[h * 64:(h + 1) * 64, :]), 16)
                vdst = bass.AP(vbuf[h % 2], 0, [[S, 128], [128, 16], [1, 128]])
                vsrc = bass.AP(vmineH[h // 4], (h % 4) * 128,
                               [[512, 128], [128 * 512, 16], [1, 128]])
                dsp.inc(sync.dma_start(out=vdst, in_=vsrc), 16)
                if h == 1:
                    emit_partner_k(0)
                    W(sync, cc, 2)
                    dPart.inc(sync.dma_start(out=vpartH[0].ap(),
                                             in_=voutH[0].ap()[T:2 * T, :],
                                             cond=(parity == 0)), 16)
                    dPart.inc(sync.dma_start(out=vpartH[0].ap(),
                                             in_=voutH[0].ap()[0:T, :],
                                             cond=(parity == 1)), 16)
                    emit_partner_v(0)
                elif h == 2:
                    emit_partner_k(1)
                    W(sync, cc, 3)
                    dPart.inc(sync.dma_start(out=vpartH[1].ap(),
                                             in_=voutH[1].ap()[T:2 * T, :],
                                             cond=(parity == 0)), 16)
                    dPart.inc(sync.dma_start(out=vpartH[1].ap(),
                                             in_=voutH[1].ap()[0:T, :],
                                             cond=(parity == 1)), 16)
                    emit_partner_v(1)
                elif h >= 3:
                    emit_partner_k(h - 1)
                    emit_partner_v(h - 1)
            emit_partner_k(7)
            emit_partner_v(7)


        @b.tensor
        def _(tensor):
            W(tensor, vP, V_QKV_END)
            for h in range(H):
                hp = (h % 2) * 64
                for s in range(2):
                    span = h * 2 + s
                    for kc in range(NKC):
                        it = span * NKC + kc
                        if it >= 2:
                            W(tensor, sEX, exp_done(it - 2))
                        if kc == 0:
                            if s == 0:
                                W(tensor, dStrP[h % 2], 32 * (h // 2 + 1))
                        if kc == 16 and s == 0:
                            W(tensor, dSpK[h % 2], 16 * (h // 2 + 1))
                            W(tensor, dSpA[h % 2], 16 * (h // 2 + 1))
                        sc = ps[kc % 2]
                        ins = None
                        for qc in range(2):
                            ins = nc.tensor.matmul(
                                sc[:, qc * 512:(qc + 1) * 512],
                                kbufs[h % 2][:, kc * 128:(kc + 1) * 128],
                                q_fm[h // 2][:,
                                             s * 1024 + qc * 512:s * 1024 + (qc + 1) * 512],
                                start=True, stop=True)
                        tSC.inc(ins, 1)
                        if kc >= 1:
                            W(tensor, sEX, exp_done(it - 1))
                            if kc == 1 and span >= 1:
                                W(tensor, sRP, span)
                            if kc == 1 and span >= 2:
                                W(tensor, vOE, span - 1)
                            ov = ps[2 + span % 2]
                            ins = None
                            for qc in range(2):
                                ins = nc.tensor.matmul(
                                    ov[:, qc * 512:(qc + 1) * 512],
                                    vbuf[h % 2][:, (kc - 1) * 128:kc * 128],
                                    pt[(it - 1) % 4][:, qc * 512:(qc + 1) * 512],
                                    start=(kc == 1), stop=False)
                            tAV.inc(ins, 1)
                    W(tensor, sEX, exp_done(span * NKC + 31))
                    ov = ps[2 + span % 2]
                    ins = None
                    for qc in range(2):
                        ins = nc.tensor.matmul(
                            ov[:, qc * 512:(qc + 1) * 512],
                            vbuf[h % 2][:, 31 * 128:32 * 128],
                            pt[(span * NKC + 31) % 4][:, qc * 512:(qc + 1) * 512],
                            start=False, stop=True)
                    tAV.inc(ins, 1)
                    ot = ps[2 + (span + 1) % 2]
                    W(tensor, vTR, span + 1)
                    ins = None
                    for qc in range(2):
                        ins = nc.tensor.matmul(ot[0:1, qc * 512:(qc + 1) * 512],
                                               ones_f[:, :],
                                               acc[:, qc * 512:(qc + 1) * 512],
                                               start=True, stop=True)
                    tSM.inc(ins, 1)
                    W(tensor, sRC, span + 1)
                    ins = None
                    for qc in range(2):
                        ins = nc.tensor.matmul(ot[:, qc * 512:(qc + 1) * 512],
                                               onesr_f[:, :],
                                               rowar[0:1, qc * 512:(qc + 1) * 512],
                                               start=True, stop=True)
                    tRP.inc(ins, 1)

        @b.scalar
        def _(scalar):
            for h in range(H):
                for s in range(2):
                    span = h * 2 + s
                    for kc in range(NKC):
                        it = span * NKC + kc
                        W(scalar, tSC, tsc_done(it))
                        m = it - 4
                        if m >= 0:
                            sp_m, j = m // NKC, m % NKC
                            W(scalar, tAV, sp_m * NKC + j + 1)
                            jodd = j if j % 2 == 1 else j + 1
                            if jodd < NKC:
                                W(scalar, vST, sp_m * 16 + (jodd + 1) // 2)
                            else:
                                W(scalar, vST, (sp_m + 1) * 16)
                        if nband_it[it]:
                            W(scalar, vBD, band_cum[it])
                        regs = span_regs[(h, s, kc)]
                        ins = None
                        for (lo, hi, cls, mc) in regs:
                            ins = scalar.activation(
                                pt[it % 4][:, lo:hi], ps[kc % 2][:, lo:hi], AF.Exp,
                                bias=bvec[:, 6 * h + cls:6 * h + cls + 1],
                                scale=f32(0.125))
                        sEX.inc(ins, 1)
                    ots = ps[2 + (span + 1) % 2]
                    W(scalar, tSM, span + 1)
                    sRC.inc(scalar.activation(rowar[0:1, 0:1024], ots[0:1, 0:1024],
                                              AF.Ln, bias=0.0, scale=f32(1.0)), 1)
                    W(scalar, tRP, span + 1)
                    if span >= 1:
                        W(scalar, vOE, span)
                    sRP.inc(scalar.activation(rcprep[:], ots[:], AF.Exp,
                                              bias=0.0, scale=f32(-1.0)), 1)

        @b.vector
        def _(vector):
            for h in range(H):
                vector.memset(master[:, h * MW:(h + 1) * MW], 0.0)
            for t in range(21):
                vector.tensor_single_scalar(rcprep[:, 0:MW], dtile[:],
                                            float(t - 10), ALU.is_equal)
                VSP(vector)
                for h in range(H):
                    vector.scalar_tensor_tensor(
                        master[:, h * MW:(h + 1) * MW], rcprep[:, 0:MW],
                        conv8[:, h * 21 + t:h * 21 + t + 1],
                        master[:, h * MW:(h + 1) * MW], ALU.mult, ALU.add)
            for h in range(H):
                vector.tensor_scalar(m16[:, h * 9:(h + 1) * 9],
                                     master[:, h * MW:h * MW + 9],
                                     srep[:, 5:6], None, ALU.mult, ALU.bypass)
                vector.tensor_scalar(mcor[:, h * 138:(h + 1) * 138],
                                     master[:, h * MW + 137:h * MW + MW],
                                     srep[:, 0:1], None, ALU.mult, ALU.bypass)
            for h in range(H):
                c0 = conv8[:, h * 21:h * 21 + 1]
                vector.tensor_scalar_mul(dtile[:, h:h + 1], c0, -1.0)
            for h in range(H):
                c0 = conv8[:, h * 21:h * 21 + 1]
                vector.scalar_tensor_tensor(dtile[:, 8 + h:9 + h], c0, 1.0,
                                            srep[:, 0:1], ALU.mult, ALU.mult)
            for h in range(H):
                c20 = conv8[:, h * 21 + 20:h * 21 + 21]
                vector.tensor_scalar_mul(dtile[:, 16 + h:17 + h], c20, -1.0)
            for h in range(H):
                vector.tensor_scalar(master[:, h * MW:(h + 1) * MW],
                                     master[:, h * MW:(h + 1) * MW],
                                     dtile[:, h:h + 1], None, ALU.add, ALU.bypass)
            for h in range(H):
                vector.tensor_scalar(m16[:, h * 9:(h + 1) * 9],
                                     m16[:, h * 9:(h + 1) * 9],
                                     dtile[:, 8 + h:9 + h], None,
                                     ALU.add, ALU.bypass)
            for h in range(H):
                vector.tensor_scalar(m16[:, h * 9:(h + 1) * 9],
                                     m16[:, h * 9:(h + 1) * 9],
                                     dtile[:, 16 + h:17 + h], None,
                                     ALU.add, ALU.bypass)
            for h in range(H):
                vector.tensor_scalar(mcor[:, h * 138:(h + 1) * 138],
                                     mcor[:, h * 138:(h + 1) * 138],
                                     dtile[:, 8 + h:9 + h], None,
                                     ALU.subtract, ALU.bypass)
            for h in range(H):
                for s in range(2):
                    span = h * 2 + s
                    for kc in range(NKC):
                        it = span * NKC + kc
                        brs = span_bands[(h, s, kc)]
                        if brs:
                            W(vector, tSC, tsc_done(it))
                            for (lo, hi, cls, mc) in brs:
                                if cls == BANDC:
                                    msl = master[:, h * MW + mc:h * MW + mc + (hi - lo)]
                                elif cls == BAND16:
                                    msl = m16[:, h * 9 + mc:h * 9 + mc + (hi - lo)]
                                else:
                                    msl = mcor[:, h * 138 + mc:h * 138 + mc + (hi - lo)]
                                vBD.inc(vector.tensor_add(ps[kc % 2][:, lo:hi],
                                                          ps[kc % 2][:, lo:hi], msl), 1)
                        if kc % 2 == 1:
                            W(vector, sEX, exp_done(it))
                            j = kc // 2
                            if kc == 1:
                                W(vector, tSM, span)
                                vST.inc(vector.tensor_add(
                                    acc[:], pt[(it - 1) % 4][:], pt[it % 4][:]), 1)
                            else:
                                vST.inc(vector.tensor_add(ttmp[j % 2][:],
                                                          pt[(it - 1) % 4][:],
                                                          pt[it % 4][:]), 1)
                                if j >= 2:
                                    vector.scalar_tensor_tensor(
                                        acc[:], ttmp[(j - 1) % 2][:], 1.0, acc[:],
                                        ALU.mult, ALU.add)
                    vTR.inc(vector.scalar_tensor_tensor(
                        acc[:], ttmp[15 % 2][:], 1.0, acc[:], ALU.mult, ALU.add), 1)
                    W(vector, sRP, span + 1)
                    W(vector, tAV, NKC * (span + 1))
                    vOE.inc(vector.tensor_mul(O_fm[h][:, s * 1024:(s + 1) * 1024],
                                              ps[2 + span % 2][:], rcprep[:]), 1)

        @b.gpsimd
        def _(gpsimd):
            W(gpsimd, tP, T_V31)
            for c in range(NCH):
                dW.inc(gpsimd.dma_start(out=warena[:, c * D:(c + 1) * D],
                                        in_=wout_h.ap()[c * 128:(c + 1) * 128, :]), 16)

    vbase, tbase = vP.n, tP.n
    t_wo = lambda i: tbase + 1 + i
    v_wo = lambda i: vbase + 1 + i
    with nc.Block() as b:
        @b.tensor
        def _(tensor):
            W(tensor, vOE, 16)
            W(tensor, sRP, 16)
            W(tensor, dW, DW_ALL)
            for i in range(32):
                dt, qc = i // 4, i % 4
                pi, po = [(0, 0), (0, 512), (1, 0), (1, 512)][i % 4]
                if i >= 4:
                    W(tensor, vP, v_wo(i - 4))
                ins = None
                for hh in range(H):
                    ins = nc.tensor.matmul(
                        ps[pi][:, po:po + 512],
                        warena[:, hh * D + dt * 128:hh * D + dt * 128 + 128],
                        O_fm[hh][:, qc * 512:(qc + 1) * 512],
                        start=(hh == 0), stop=(hh == 7))
                tP.inc(ins, 1)

        @b.vector
        def _(vector):
            for i in range(32):
                dt, qc = i // 4, i % 4
                pi, po = [(0, 0), (0, 512), (1, 0), (1, 512)][i % 4]
                W(vector, tP, t_wo(i))
                vP.inc(vector.scalar_tensor_tensor(
                    X[dt][:, qc * 512:(qc + 1) * 512], ps[pi][:, po:po + 512],
                    bout_sb[:, dt:dt + 1], X[dt][:, qc * 512:(qc + 1) * 512],
                    ALU.add, ALU.add), 1)
    T_WOUT_END = tP.n
    V_WOUT_END = vP.n

    ln3 = emit_ln('xq')
    ffn2 = emit_ffn(store_y=True)
    V_FINAL = vP.n

    return nc


def kernel(**inputs):
    x = np.asarray(inputs["L0_attn_x"])
    wqk = np.ascontiguousarray(inputs["L0_attn_W_qk"], dtype=np.float32)
    wv = np.ascontiguousarray(inputs["L0_attn_W_v"], dtype=np.float32)
    wout = np.ascontiguousarray(inputs["L0_attn_W_out"], dtype=np.float32)
    bout = np.asarray(inputs["L0_attn_b_out"], dtype=np.float32)
    conv = np.ascontiguousarray(inputs["L0_attn_conv"], dtype=np.float32)
    wffn = np.ascontiguousarray(inputs["L0_ffn_kernel"])
    fbias = np.asarray(inputs["L0_ffn_bias"])
    sin = np.asarray(inputs["L0_ffn_input_scale"], dtype=np.float32).reshape(1)
    sk = np.asarray(inputs["L0_ffn_kernel_scale"], dtype=np.float32).reshape(1)

    bout2d = np.ascontiguousarray(bout.reshape(8, 128).T)
    fbias2d = np.ascontiguousarray(fbias.reshape(8, 128).T)

    nc = build_nc()
    in_maps = []
    for c in range(8):
        bq, hh = c // 2, c % 2
        xs = np.ascontiguousarray(x[bq, hh * T:(hh + 1) * T, :].T)
        scal = np.array([[float(c % 2), float(sin[0]), float(sk[0])]], dtype=np.float32)
        in_maps.append({
            "x": xs, "wqk": wqk, "wv": wv, "wout": wout, "bout2d": bout2d,
            "wffn": wffn, "fbias2d": fbias2d, "scal": scal, "conv": conv,
        })
    res = run_bass_kernel_spmd(nc, in_maps, core_ids=list(range(8)))
    out = np.empty((4, S, D), dtype=np.float32)
    for c in range(8):
        bq, hh = c // 2, c % 2
        out[bq, hh * T:(hh + 1) * T, :] = res.results[c]["y"].T
    return out
